# version 06
# baseline (speedup 1.0000x reference)
import numpy as np
from contextlib import ExitStack

import concourse.bass as bass
import concourse.mybir as mybir
from concourse.bass_utils import run_bass_kernel_spmd

F32 = mybir.dt.float32
BF16 = mybir.dt.bfloat16
ALU = mybir.AluOpType
AF = mybir.ActivationFunctionType
AX = mybir.AxisListType

T, D = 2048, 1024
H, DH = 4, 256
C = 128
NCH = T // C
SLAB = 512
NSLAB = T // SLAB
CPS = SLAB // C
NT = D // 128
NB = D // 32
NPOW = 20
ELO = -13
LN2 = float(np.log(2.0))
TAU2 = [0.0625, 0.5625, 1.5625, 3.0625, 6.25, 12.25, 25.0]
NEWTON_ITERS = 6


def build_program():
    nc = bass.Bass("TRN2", target_bir_lowering=False)
    nc.detect_race_conditions = False
    dram = {}

    def din(name, shape, dtype=F32):
        dram[name] = nc.dram_tensor(name, list(shape), dtype, kind="ExternalInput")

    din("x", (T, D))
    for w in ("wq", "wk", "wv", "wo", "fw"):
        din(w, (D, D))
    din("fbias", (1, D))
    din("dw", (D, 32))
    din("da", (128, 1))
    din("onw", (128, DH))
    din("convdiag", (3 * NT * 4 * 128, 128))
    din("identb", (128, 128), BF16)
    din("lmask", (128, 128), BF16)
    din("umaski", (128, 128), BF16)
    din("imat", (128, 128), BF16)
    din("i2mat", (128, 128), BF16)
    din("powcube", (128, NB * NPOW), BF16)
    din("onesb", (1, SLAB), BF16)
    dram["out"] = nc.dram_tensor("out", [T, D], F32, kind="ExternalOutput")
    dram["wbf"] = nc.dram_tensor("wbf", [4 * NT * 128, D], BF16, kind="Internal")

    stack = ExitStack()

    def sb(name, shape, dtype=F32):
        return stack.enter_context(nc.sbuf_tensor(name, list(shape), dtype))

    xres = sb("xres", (128, NCH * D))
    wqT = sb("wqT", (128, NT * D), BF16)
    wstage = sb("wstage", (128, D))
    cstage = wstage[:, 0:512]
    brow = wstage[0:1, 0:D]
    wcast = sb("wcast", (128, D), BF16)
    ws = [sb(f"ws{m}", (128, 2 * D), BF16) for m in range(3)]
    wo_buf = sb("wo_buf", (128, NT * D), BF16)
    qkv = sb("qkv", (128, 3 * NT * SLAB), BF16)
    cdiag = sb("cdiag", (128, 3 * NT * 4 * 128), BF16)
    ptmp = sb("ptmp", (128, SLAB + 3), BF16)
    phalo = sb("phalo", (128, 3 * NT * 3), BF16)
    silu_s = sb("silu_s", (128, SLAB))
    yq_y = sb("yq_y", (128, D))
    yq_xs = sb("yq_xs", (128, D))
    yq_t1 = sb("yq_t1", (128, D))
    yq_t2 = sb("yq_t2", (128, D))
    yq_sq = sb("yq_sq", (128, D))
    yq_cube = sb("yq_cube", (128, NB * NPOW))
    yq_out = sb("yq_out", (128, D), BF16)
    xqTc = sb("xqTc", (128, NT * 128), BF16)
    yq_sgn = yq_y
    pool_base = nc.lookup_mloc(yq_y).addr

    def sba(name, shape, dtype, off):
        return nc.alloc_sbuf_tensor_at(name, list(shape), dtype,
                                       offset=pool_base + off)

    xnT = sba("xnT", (128, NT * SLAB), BF16, 0)
    mk_ot = sba("mk_ot", (128, CPS * D), BF16, 8192)
    xnc = sba("xnc", (128, D), BF16, 16384)
    mk_tm = sba("mk_tm", (128, 3 * DH), BF16, 18432)
    mk_m = sba("mk_m", (128, 4 * 128), BF16, 19968)
    mk_x = sba("mk_x", (128, 3 * 128), BF16, 20992)
    mk_s = sba("mk_s", (128, 2 * DH), BF16, 21760)
    mk_w = sba("mk_w", (128, DH), BF16, 22784)
    mk_r = sba("mk_r", (128, DH), BF16, 23296)
    mk_p = sba("mk_p", (128, 128), BF16, 23808)
    dumpA = sba("dumpA", (128, D), F32, 8192)
    mk_dump = sba("mk_dump", (128, DH), F32, 16384)
    silu_r = dumpA
    bias_b = sb("bias_b", (1, D), BF16)
    onesb_t = sb("onesb_t", (1, SLAB), BF16)
    onw_t = sb("onw_t", (128, DH))
    cst = sb("cst", (128, 16))
    identb_t = sb("identb_t", (128, 128), BF16)
    lmask_t = sb("lmask_t", (128, 128), BF16)
    umaski_t = sb("umaski_t", (128, 128), BF16)
    imat_t = sb("imat_t", (128, 128), BF16)
    i2mat_t = sb("i2mat_t", (128, 128), BF16)
    powcube_t = sb("powcube_t", (128, NB * NPOW), BF16)
    da_t = sb("da_t", (128, 1))
    dwcol = sb("dwcol", (128, 32))
    s_sb = [sb(f"s_sb{i}", (128, 2 * DH)) for i in range(4)]
    sm = sb("sm", (128, 160))
    smw = sb("smw", (128, 96))

    ps_mm = stack.enter_context(nc.psum_tensor("ps_mm", [128, 1024], F32))
    ps_w = stack.enter_context(nc.psum_tensor("ps_w", [128, 512], F32))
    ps_t = stack.enter_context(nc.psum_tensor("ps_t", [128, 1024], BF16))

    S_ = dict(mean=0, var=1, rstd=2, ss=5, tmp=6, tmp2=7,
              amax=(8, 40), n=(40, 72), rinv=(72, 104), gb=(104, 136),
              rbias=136, gbias=137, kss=140, sk=141, sk2=142, nsk2=143,
              qss=144, sq=145, oss=146, zz=148, pad=150)

    def smc(k):
        v = S_[k]
        if isinstance(v, tuple):
            return sm[:, v[0]:v[1]]
        return sm[:, v:v + 1]

    CVAL = {"ln2": LN2, "nln2": -LN2, "nhalf": -0.5, "none": -1.0,
            "rw": (2.0 - ELO) * LN2,
            "gw": (ELO - 2.0) * LN2 + float(np.log(0.25)),
            "ra": (2.0 - ELO) * LN2,
            "lnq16": -float(np.log(16.0)), "zero": 0.0,
            "one": 1.0, "eps5": 1e-5}
    CIDX = {k: i for i, k in enumerate(CVAL)}

    def cc(k):
        return cst[:, CIDX[k]:CIDX[k] + 1]

    blk = stack.enter_context(nc.Block())
    ops = {e: [] for e in ("sync", "gpsimd", "vector", "scalar", "tensor")}
    sems = {e: stack.enter_context(nc.semaphore("s_" + e)) for e in ops}
    cnt = {e: 0 for e in ops}
    COMPUTE = ("vector", "scalar", "tensor")

    def E(eng, fn, waits=(), inc=False):
        ops[eng].append((fn, tuple(w for w in waits if w is not None), inc))
        if inc:
            cnt[eng] += 16 if eng in ("sync", "gpsimd") else 1
            return (eng, cnt[eng])
        return None

    def tok(eng):
        return (eng, cnt[eng])

    def dma(eng, dst, src, waits=()):
        return E(eng, lambda e, d=dst, s=src: e.dma_start(out=d, in_=s),
                 waits=waits, inc=True)

    for nm, dst in (("identb", identb_t[:]), ("lmask", lmask_t[:]),
                    ("umaski", umaski_t[:]), ("imat", imat_t[:]),
                    ("i2mat", i2mat_t[:]), ("powcube", powcube_t[:]),
                    ("da", da_t[:]), ("onw", onw_t[:]), ("onesb", onesb_t[:]),
                    ("fbias", brow)):
        dma("sync", dst, dram[nm].ap())
    consts_tok = tok("sync")

    for k in CVAL:
        E("vector", lambda e, a=cc(k), vv=float(CVAL[k]): e.memset(a, vv))
    E("vector", lambda e: e.memset(phalo[:], 0.0))
    E("vector", lambda e: e.memset(sm[:], 0.0), inc=True)

    E("vector", lambda e: e.tensor_scalar(bias_b[:], brow, 0.5, None, ALU.mult),
      waits=[consts_tok], inc=True)
    bias_tok = tok("vector")

    E("vector", lambda e: e.tensor_scalar(smc("tmp"), da_t[:], 8.0, None, ALU.mult),
      waits=[consts_tok], inc=True)
    t_ = tok("vector")
    E("scalar", lambda e: e.activation(smc("tmp2"), smc("tmp"), AF.Exp), waits=[t_],
      inc=True)
    t_ = tok("scalar")
    E("vector", lambda e: e.tensor_scalar(smc("tmp2"), smc("tmp2"), 1.0, None,
                                          ALU.add), waits=[t_], inc=True)
    t_ = tok("vector")
    E("scalar", lambda e: e.activation(smc("tmp"), smc("tmp2"), AF.Ln),
      waits=[t_])
    E("scalar", lambda e: e.activation(smc("tmp2"), smc("tmp"), AF.Exp, bias=cc("zero"),
                                       scale=cc("none")), inc=True)
    t_ = tok("scalar")
    GB0 = (ELO - 2.0) * LN2 + float(np.log(0.5))
    E("vector", lambda e: e.tensor_scalar(
        smc("rbias"), smc("tmp2"), LN2, CVAL["ra"] - 0.5 * LN2,
        ALU.mult, ALU.add), waits=[t_])
    E("vector", lambda e: e.tensor_scalar(
        smc("gbias"), smc("tmp2"), -LN2, GB0 + 0.5 * LN2,
        ALU.mult, ALU.add), inc=True)
    actbias_tok = tok("vector")

    cd_prev = [bias_tok, bias_tok]
    for g in range(3 * NT):
        b = g % 2
        cs = wstage[:, b * 512:(b + 1) * 512]
        src = dram["convdiag"].ap().rearrange("(g p) n -> p g n", p=128)
        t_l = dma("sync", cs.rearrange("p (g n) -> p g n", n=128),
                  src[:, g * 4:(g + 1) * 4, :], waits=[cd_prev[b]])
        E("vector", lambda e, g=g, cs=cs: e.tensor_copy(
            cdiag[:, g * 512:(g + 1) * 512], cs), waits=[t_l], inc=True)
        cd_prev[b] = tok("vector")
    cdiag_tok = tok("vector")

    x_tok = []
    for c_ in range(NCH):
        x_tok.append(dma("sync", xres[:, c_ * D:(c_ + 1) * D],
                         dram["x"].ap()[c_ * 128:(c_ + 1) * 128, :]))

    last_store = None
    wstage_gate = [bias_tok, cdiag_tok]
    for mi, w in enumerate(("wq", "wk", "wv", "wo")):
        for k in range(NT):
            t_l = dma("sync", wstage[:], dram[w].ap()[k * 128:(k + 1) * 128, :],
                      waits=[last_store] + wstage_gate)
            wstage_gate = []
            E("vector", lambda e: e.tensor_copy(wcast[:], wstage[:]),
              waits=[t_l], inc=True)
            t_c = tok("vector")
            last_store = dma(
                "gpsimd",
                dram["wbf"].ap()[(mi * NT + k) * 128:(mi * NT + k + 1) * 128, :],
                wcast[:], waits=[t_c])
    wbf_done = tok("gpsimd")

    prev_q_s = [None]

    def quant_tile(y_ap, out_bf, rbias_col, gbias_col, waits, wq_fblocks=None):
        w = list(waits) + [prev_q_s[0]]
        amax = smc("amax"); n_ = smc("n"); rinv = smc("rinv"); gb = smc("gb")
        y3 = y_ap.rearrange("p (b e) -> p b e", e=32)
        E("vector", lambda e: e.tensor_reduce(
            amax.rearrange("p (b o) -> p b o", o=1), y3, AX.X, ALU.max,
            apply_absolute_value=True), waits=w, inc=True)
        cc3 = yq_cube[:].rearrange("p (b k) -> p b k", k=NPOW)
        pc3 = powcube_t[:].rearrange("p (b k) -> p b k", k=NPOW)
        ab = amax.rearrange("p (b o) -> p b o", o=1).broadcast_to([128, NB, NPOW])
        E("vector", lambda e: e.scalar_tensor_tensor(cc3, pc3, 1.0, ab,
                                                     ALU.mult, ALU.is_le))
        E("vector", lambda e: e.tensor_reduce(
            n_.rearrange("p (b o) -> p b o", o=1), cc3, AX.X, ALU.add), inc=True)
        vt = tok("vector")
        E("scalar", lambda e: e.activation(rinv, n_, AF.Exp, bias=rbias_col,
                                           scale=cc("nln2")), waits=[vt])
        E("scalar", lambda e: e.activation(gb, n_, AF.Exp, bias=gbias_col,
                                           scale=cc("ln2")), inc=True)
        st = tok("scalar")
        if wq_fblocks is not None:
            finv, fmul = wq_fblocks
            E("vector", lambda e: e.tensor_tensor(rinv, rinv, finv, ALU.mult),
              waits=[st])
            E("vector", lambda e: e.tensor_tensor(gb, gb, fmul, ALU.mult))
            E("vector", lambda e: e.memset(smc("pad"), 0.0))
            E("vector", lambda e: e.memset(smc("pad"), 0.0), inc=True)
            st = tok("vector")
        xs3 = yq_xs[:].rearrange("p (b e) -> p b e", e=32)
        rb = rinv.rearrange("p (b o) -> p b o", o=1).broadcast_to([128, NB, 32])
        gbb = gb.rearrange("p (b o) -> p b o", o=1).broadcast_to([128, NB, 32])
        E("vector", lambda e: e.scalar_tensor_tensor(xs3, y3, 1.0, rb,
                                                     ALU.mult, ALU.mult),
          waits=[st], inc=True)
        vt = tok("vector")
        E("scalar", lambda e: e.activation(yq_sq[:], yq_xs[:], AF.Square),
          waits=[vt])
        E("scalar", lambda e: e.activation(yq_sgn[:], yq_xs[:], AF.Sign), inc=True)
        st = tok("scalar")
        prev_q_s[0] = st
        E("vector", lambda e: e.tensor_scalar(yq_t1[:], yq_sq[:], TAU2[0], None,
                                              ALU.is_ge), waits=[st])
        for j in (1, 2, 3):
            E("vector", lambda e, j=j: e.scalar_tensor_tensor(
                yq_t1[:], yq_sq[:], TAU2[j], yq_t1[:], ALU.is_ge, ALU.add))
        E("vector", lambda e: e.tensor_scalar(yq_t2[:], yq_sq[:], TAU2[4], None,
                                              ALU.is_ge))
        E("vector", lambda e: e.scalar_tensor_tensor(
            yq_t2[:], yq_sq[:], TAU2[5], yq_t2[:], ALU.is_ge, ALU.add))
        E("vector", lambda e: e.scalar_tensor_tensor(
            yq_t2[:], yq_t2[:], 2.0, yq_t1[:], ALU.mult, ALU.add))
        E("vector", lambda e: e.tensor_scalar(yq_xs[:], yq_sq[:], TAU2[6], None,
                                              ALU.is_ge))
        E("vector", lambda e: e.scalar_tensor_tensor(
            yq_t1[:], yq_xs[:], 4.0, yq_t2[:], ALU.mult, ALU.add))
        E("vector", lambda e: e.tensor_tensor(yq_t2[:], yq_t1[:], yq_sgn[:],
                                              ALU.mult))
        o3 = out_bf.rearrange("p (b e) -> p b e", e=32)
        t23 = yq_t2[:].rearrange("p (b e) -> p b e", e=32)
        E("vector", lambda e: e.scalar_tensor_tensor(o3, t23, 1.0, gbb,
                                                     ALU.mult, ALU.mult),
          inc=True)
        return tok("vector")

    ps_t_free = None
    prev_qt = [None]
    for o in range(NT):
        t_fw = dma("sync", yq_y[:], dram["fw"].ap()[o * 128:(o + 1) * 128, :],
                   waits=[prev_q_s[0], prev_qt[0]])
        t_dw = dma("sync", dwcol[:], dram["dw"].ap()[o * 128:(o + 1) * 128, :],
                   waits=[prev_q_s[0]])
        E("vector", lambda e: e.tensor_scalar(smw[:, 0:32], dwcol[:], 8.0, None,
                                              ALU.mult), waits=[t_dw], inc=True)
        vt = tok("vector")
        E("scalar", lambda e: e.activation(smw[:, 64:96], smw[:, 0:32], AF.Exp),
          waits=[vt], inc=True)
        st = tok("scalar")
        E("vector", lambda e: e.tensor_scalar(smw[:, 32:64], smw[:, 64:96], 1.0,
                                              None, ALU.add), waits=[st],
          inc=True)
        vt = tok("vector")
        E("scalar", lambda e: e.activation(smw[:, 0:32], smw[:, 32:64], AF.Ln),
          waits=[vt])
        E("scalar", lambda e: e.activation(smw[:, 32:64], smw[:, 0:32], AF.Exp,
                                           bias=cc("zero"), scale=cc("none")), inc=True)
        st = tok("scalar")
        E("vector", lambda e: e.tensor_scalar(smw[:, 32:64], smw[:, 32:64], -LN2,
                                              0.5 * LN2, ALU.mult, ALU.add),
          waits=[st], inc=True)
        vt = tok("vector")
        E("scalar", lambda e: e.activation(smw[:, 0:32], smw[:, 32:64], AF.Exp,
                                           bias=cc("zero"), scale=cc("none")),
          waits=[vt])
        E("scalar", lambda e: e.activation(smw[:, 64:96], smw[:, 32:64], AF.Exp),
          inc=True)
        st = tok("scalar")
        qt = quant_tile(yq_y[:], yq_out[:], cc("rw"), cc("gw"),
                        [t_fw, st, consts_tok],
                        wq_fblocks=(smw[:, 0:32], smw[:, 64:96]))
        prev_qt[0] = qt
        for k in range(NT):
            E("tensor", lambda e, k=k: e.transpose(
                ps_t[:, k * 128:(k + 1) * 128], yq_out[:, k * 128:(k + 1) * 128],
                identb_t[:]),
              waits=[qt, consts_tok, ps_t_free] if k == 0 else (),
              inc=(k == NT - 1))
        tt = tok("tensor")
        E("vector", lambda e, o=o: e.tensor_copy(
            wqT[:, o * D:(o + 1) * D], ps_t[:]), waits=[tt], inc=True)
        ps_t_free = tok("vector")
    wqT_tok = ps_t_free

    prev_ln_s = [None]

    def ln_stats(xc_ap, waits, dump):
        E("vector", lambda e: e.tensor_reduce(
            smc("mean").rearrange("p (a o) -> p a o", a=1),
            xc_ap.rearrange("p (o n) -> p o n", o=1), AX.X, ALU.add),
          waits=list(waits) + [prev_ln_s[0]], inc=True)
        vt0 = tok("vector")
        E("scalar", lambda e: e.activation(dump, xc_ap, AF.Square,
                                           accum_out=smc("ss")),
          waits=[vt0], inc=True)
        st0 = tok("scalar")
        E("vector", lambda e: e.tensor_scalar(smc("var"), smc("ss"), 1.0 / D, None,
                                              ALU.mult), waits=[st0])
        E("vector", lambda e: e.memset(smc("pad"), 0.0))
        E("vector", lambda e: e.tensor_scalar(smc("mean"), smc("mean"), 1.0 / D,
                                              None, ALU.mult), inc=True)
        vt1 = tok("vector")
        E("scalar", lambda e: e.activation(smc("tmp"), smc("mean"), AF.Square),
          waits=[vt1], inc=True)
        st1 = tok("scalar")
        E("vector", lambda e: e.scalar_tensor_tensor(
            smc("var"), smc("tmp"), -1.0, smc("var"), ALU.mult, ALU.add),
          waits=[st1], inc=True)
        vt2 = tok("vector")
        E("vector", lambda e: e.memset(smc("pad"), 0.0))
        E("scalar", lambda e: e.activation(smc("tmp2"), smc("var"), AF.Ln,
                                           bias=cc("eps5"), scale=cc("one")),
          waits=[vt2])
        E("scalar", lambda e: e.activation(smc("rstd"), smc("tmp2"), AF.Exp,
                                           bias=cc("zero"), scale=cc("nhalf")), inc=True)
        st = tok("scalar")
        prev_ln_s[0] = st
        return st

    def ffn_phase(prev_resid, store_out):
        resid_toks = []
        nonlocal ps_t_free
        for c_ in range(NCH):
            xc_ap = xres[:, c_ * D:(c_ + 1) * D]
            base_w = [x_tok[c_], wqT_tok] + ([prev_resid[c_]] if prev_resid else [])
            lt = ln_stats(xc_ap, base_w, dump=yq_t1[:])
            E("vector", lambda e, xc_ap=xc_ap: e.tensor_scalar(
                yq_y[:], xc_ap, smc("mean"), smc("rstd"), ALU.subtract,
                ALU.mult), waits=[lt], inc=True)
            vt = tok("vector")
            qt = quant_tile(yq_y[:], yq_out[:], smc("rbias"), smc("gbias"), [vt])
            for k in range(NT):
                E("tensor", lambda e, k=k: e.transpose(
                    ps_t[:, k * 128:(k + 1) * 128],
                    yq_out[:, k * 128:(k + 1) * 128], identb_t[:]),
                  waits=[qt, ps_t_free] if k == 0 else (), inc=(k == NT - 1))
            tt = tok("tensor")
            E("vector", lambda e: e.tensor_copy(xqTc[:], ps_t[:]), waits=[tt],
              inc=True)
            evt = tok("vector")
            ps_t_free = evt
            wq3 = wqT[:].rearrange("p (o n) -> p o n", n=D)
            for sl in range(2):
                for k in range(NT):
                    E("tensor", lambda e, sl=sl, k=k: e.matmul(
                        ps_mm[:, sl * 512:(sl + 1) * 512],
                        xqTc[:, k * 128:(k + 1) * 128],
                        wq3[:, 4 * sl:4 * sl + 4, k * 128:(k + 1) * 128],
                        start=(k == 0), stop=False),
                      waits=[evt, bias_tok] if (sl == 0 and k == 0) else ())
                E("tensor", lambda e, sl=sl: e.matmul(
                    ps_mm[:, sl * 512:(sl + 1) * 512], onesb_t[0:1, 0:128],
                    bias_b[0:1, sl * 512:(sl + 1) * 512], start=False, stop=True),
                  inc=(sl == 1))
            mt = tok("tensor")
            E("vector", lambda e, xc_ap=xc_ap: e.tensor_tensor(
                xc_ap, xc_ap, ps_mm[:], ALU.add), waits=[mt], inc=True)
            resid_toks.append(tok("vector"))
            if store_out:
                dma("gpsimd", dram["out"].ap()[c_ * 128:(c_ + 1) * 128, :], xc_ap,
                    waits=[resid_toks[-1]])
        return resid_toks

    f1_resid = ffn_phase(None, store_out=False)

    resid2 = []
    ws_free = [None, None, None]
    wo_tok = dma("sync", wo_buf[:].rearrange("p (k n) -> p k n", n=D),
                 dram["wbf"].ap()[3 * NT * 128: 4 * NT * 128, :]
                 .rearrange("(k p) n -> p k n", p=128), waits=[wbf_done])
    ps_w_free = None
    head_gate = None
    supd_tok = [None] * H
    supd_gate = [None]
    for s in range(NSLAB):
        for ci in range(CPS):
            c_ = s * CPS + ci
            xc_ap = xres[:, c_ * D:(c_ + 1) * D]
            lt = ln_stats(xc_ap, [f1_resid[c_]], dump=dumpA[:])
            E("vector", lambda e, xc_ap=xc_ap: e.tensor_scalar(
                xnc[:], xc_ap, smc("mean"), smc("rstd"), ALU.subtract, ALU.mult),
              waits=[lt], inc=True)
            vt = tok("vector")
            for k in range(NT):
                E("tensor", lambda e, k=k: e.transpose(
                    ps_t[:, k * 128:(k + 1) * 128], xnc[:, k * 128:(k + 1) * 128],
                    identb_t[:]),
                  waits=[vt, ps_t_free] if k == 0 else (), inc=(k == NT - 1))
            tt = tok("tensor")
            E("vector", lambda e, ci=ci: e.tensor_copy(
                xnT[:].rearrange("p (k n) -> p k n", n=SLAB)
                [:, :, ci * 128:(ci + 1) * 128],
                ps_t[:].rearrange("p (k n) -> p k n", n=128)), waits=[tt],
              inc=True)
            ps_t_free = tok("vector")
        xn_tok = tok("vector")

        for mi in range(3):
            for dt in range(NT):
                b = dt % 2
                t_wl = dma("sync",
                           ws[mi][:, b * D:(b + 1) * D]
                           .rearrange("p (k n) -> p k n", n=128),
                           dram["wbf"].ap()
                           .rearrange("(q p) n -> p q n", p=128)
                           [:, mi * NT:(mi + 1) * NT, dt * 128:(dt + 1) * 128],
                           waits=[wbf_done, ws_free[mi]])
                for k in range(NT):
                    E("tensor", lambda e, k=k, b=b, mi=mi: e.matmul(
                        ps_w[:], ws[mi][:, b * D + k * 128: b * D + (k + 1) * 128],
                        xnT[:, k * SLAB:(k + 1) * SLAB],
                        start=(k == 0), stop=(k == NT - 1)),
                      waits=[t_wl, xn_tok, ps_w_free] if k == 0 else (),
                      inc=(k == NT - 1))
                pt = tok("tensor")
                if dt % 2 == 1:
                    ws_free[mi] = pt
                hcol = (mi * NT + dt) * 3
                E("vector", lambda e, hcol=hcol: e.tensor_copy(
                    ptmp[:, 0:3], phalo[:, hcol:hcol + 3]), waits=[pt])
                E("vector", lambda e: e.tensor_copy(ptmp[:, 3:SLAB + 3], ps_w[:]),
                  inc=True)
                evt = tok("vector")
                E("vector", lambda e, hcol=hcol: e.tensor_copy(
                    phalo[:, hcol:hcol + 3], ps_w[:, SLAB - 3:SLAB]), inc=True)
                g = (mi * NT + dt) * 512
                for i in range(4):
                    E("tensor", lambda e, i=i, g=g: e.matmul(
                        ps_w[:], cdiag[:, g + i * 128: g + (i + 1) * 128],
                        ptmp[:, i:i + SLAB], start=(i == 0), stop=(i == 3)),
                      waits=[evt, cdiag_tok] if i == 0 else (), inc=(i == 3))
                ct = tok("tensor")
                E("scalar", lambda e: e.activation(silu_s[:], ps_w[:], AF.Exp,
                                                   bias=cc("zero"), scale=cc("none")),
                  waits=[ct], inc=True)
                st = tok("scalar")
                E("vector", lambda e: e.tensor_scalar(silu_s[:], silu_s[:], 1.0,
                                                      None, ALU.add),
                  waits=[st], inc=True)
                vt_s = tok("vector")
                E("scalar", lambda e: e.activation(silu_r[:, 0:SLAB], silu_s[:],
                                                   AF.Ln), waits=[vt_s])
                E("scalar", lambda e: e.activation(silu_s[:], silu_r[:, 0:SLAB],
                                                   AF.Exp, bias=cc("zero"),
                                                   scale=cc("none")), inc=True)
                st2 = tok("scalar")
                qoff = (mi * NT + dt) * SLAB
                E("vector", lambda e, qoff=qoff: e.tensor_tensor(
                    qkv[:, qoff:qoff + SLAB], ps_w[:], silu_s[:], ALU.mult),
                  waits=[st2], inc=True)
                ps_w_free = tok("vector")
        qkv_tok = tok("vector")

        for ci in range(CPS):
            c_ = s * CPS + ci
            coff = ci * 128
            for h in range(H):
                dts = (2 * h, 2 * h + 1)

                def qv(mat, dtl, n0, n1):
                    base = (mat * NT + dtl) * SLAB
                    return qkv[:, base + n0:base + n1]

                tpw = [qkv_tok, head_gate, ps_t_free]
                nt_i = 0
                for bi, dtl in enumerate(dts):
                    for mat, base in ((1, 0), (2, 256), (0, 512)):
                        dst_ap = ps_t[:, base + bi * 128: base + (bi + 1) * 128]
                        src_ap = qv(mat, dtl, coff, coff + 128)
                        E("tensor",
                          lambda e, d=dst_ap, s_=src_ap:
                          e.transpose(d, s_, identb_t[:]),
                          waits=tpw if nt_i == 0 else (), inc=(nt_i == 5))
                        nt_i += 1
                tt = tok("tensor")
                E("vector", lambda e: e.tensor_copy(mk_tm[:, 0:256],
                                                    ps_t[:, 0:256]),
                  waits=[tt, supd_tok[h]])
                E("vector", lambda e: e.tensor_copy(mk_tm[:, 256:512],
                                                    ps_t[:, 512:768]), inc=True)
                evt0 = tok("vector")
                E("scalar", lambda e: e.activation(mk_dump[:, 0:DH],
                                                   mk_tm[:, 0:256], AF.Square,
                                                   accum_out=smc("kss")),
                  waits=[evt0])
                E("scalar", lambda e: e.activation(mk_dump[:, 0:DH],
                                                   mk_tm[:, 256:512], AF.Square,
                                                   accum_out=smc("qss")),
                  inc=True)
                sq_tok = tok("scalar")
                E("vector", lambda e: e.tensor_scalar(smc("kss"), smc("kss"), 1.0,
                                                      1e-6, ALU.mult, ALU.add),
                  waits=[sq_tok])
                E("vector", lambda e: e.tensor_scalar(smc("qss"), smc("qss"), 1.0,
                                                      1e-6, ALU.mult, ALU.add),
                  inc=True)
                vt = tok("vector")
                E("scalar", lambda e: e.activation(smc("tmp"), smc("kss"), AF.Ln),
                  waits=[vt])
                E("scalar", lambda e: e.activation(smc("sk"), smc("tmp"), AF.Exp,
                                                   bias=cc("zero"), scale=cc("nhalf")))
                E("scalar", lambda e: e.activation(smc("tmp2"), smc("qss"),
                                                   AF.Ln))
                E("scalar", lambda e: e.activation(smc("sq"), smc("tmp2"), AF.Exp,
                                                   bias=cc("lnq16"),
                                                   scale=cc("nhalf")), inc=True)
                st = tok("scalar")
                E("vector", lambda e: e.tensor_tensor(smc("sk2"), smc("sk"),
                                                      smc("sk"), ALU.mult),
                  waits=[st])
                E("vector", lambda e: e.scalar_tensor_tensor(
                    smc("nsk2"), smc("sk"), -1.0, smc("sk"), ALU.mult, ALU.mult))
                E("vector", lambda e: e.tensor_scalar(
                    mk_tm[:, 512:768], ps_t[:, 256:512], smc("sk"), None,
                    ALU.mult), inc=True)
                vt_tm = tok("vector")

                for gi, (rmat, dst0) in enumerate(((1, 0), (0, 128))):
                    for bi, dtl in enumerate(dts):
                        d_ap = ps_w[:, dst0:dst0 + 128]
                        l_ap = qv(1, dtl, coff, coff + 128)
                        r_ap = qv(rmat, dtl, coff, coff + 128)
                        E("tensor", lambda e, d=d_ap, l=l_ap, r=r_ap, bi=bi:
                          e.matmul(d, l, r, start=(bi == 0), stop=(bi == 1)),
                          waits=[vt_tm, ps_w_free]
                          if (gi == 0 and bi == 0) else (),
                          inc=(gi == 1 and bi == 1))
                gt = tok("tensor")
                E("vector", lambda e: e.tensor_scalar(
                    mk_m[:, 0:128], ps_w[:, 0:128], smc("sk2"), None, ALU.mult),
                  waits=[gt])
                E("vector", lambda e: e.tensor_tensor(
                    mk_m[:, 0:128], mk_m[:, 0:128], lmask_t[:], ALU.mult))
                E("vector", lambda e: e.tensor_copy(mk_p[:], ps_w[:, 128:256]))
                E("vector", lambda e: e.tensor_tensor(
                    mk_p[:], mk_p[:], umaski_t[:], ALU.mult), inc=True)
                at_tok = tok("vector")
                E("tensor", lambda e: e.transpose(ps_t[:, 768:896],
                                                  mk_m[:, 0:128], identb_t[:]),
                  waits=[at_tok], inc=True)
                tt2 = tok("tensor")
                E("vector", lambda e: e.tensor_copy(mk_m[:, 128:256],
                                                    ps_t[:, 768:896]),
                  waits=[tt2])
                E("vector", lambda e: e.tensor_tensor(
                    mk_m[:, 384:512], mk_m[:, 128:256], imat_t[:], ALU.add))
                E("vector", lambda e: e.tensor_tensor(
                    mk_x[:, 0:128], imat_t[:], mk_m[:, 0:128], ALU.subtract))
                E("vector", lambda e: e.tensor_tensor(
                    mk_x[:, 128:256], imat_t[:], mk_m[:, 128:256], ALU.subtract),
                  inc=True)
                x0_tok = tok("vector")

                if c_ > 0:
                    E("vector", lambda e, h=h: e.tensor_copy(
                        mk_s[:], s_sb[h][:]), waits=[supd_tok[h]], inc=True)
                else:
                    E("vector", lambda e: e.memset(mk_s[:], 0.0), inc=True)
                s_tok = tok("vector")

                for bi, dtl in enumerate(dts):
                    l_ap = qv(1, dtl, coff, coff + 128)
                    E("tensor", lambda e, l=l_ap, bi=bi: e.matmul(
                        ps_w[:, 0:256], l, mk_s[:, bi * 256:(bi + 1) * 256],
                        start=(bi == 0), stop=(bi == 1)),
                      waits=[s_tok, at_tok] if bi == 0 else (), inc=(bi == 1))
                kst = tok("tensor")
                E("vector", lambda e: e.scalar_tensor_tensor(
                    mk_r[:], ps_w[:, 0:256], smc("nsk2"), mk_tm[:, 512:768],
                    ALU.mult, ALU.add), waits=[kst], inc=True)
                r_tok = tok("vector")

                prev = x0_tok
                for it in range(NEWTON_ITERS):
                    E("tensor", lambda e: e.matmul(
                        ps_w[:, 256:384], mk_m[:, 384:512], mk_x[:, 0:128],
                        start=True, stop=True), waits=[prev, r_tok], inc=True)
                    pt2 = tok("tensor")
                    E("vector", lambda e: e.scalar_tensor_tensor(
                        mk_x[:, 256:384], ps_w[:, 256:384], -1.0, i2mat_t[:],
                        ALU.mult, ALU.add), waits=[pt2], inc=True)
                    et = tok("vector")
                    E("tensor", lambda e: e.matmul(
                        ps_w[:, 256:384], mk_x[:, 128:256], mk_x[:, 256:384],
                        start=True, stop=True), waits=[et])
                    E("tensor", lambda e: e.matmul(
                        ps_w[:, 384:512], mk_x[:, 256:384], mk_x[:, 128:256],
                        start=True, stop=True), inc=True)
                    xt2 = tok("tensor")
                    E("vector", lambda e: e.tensor_copy(
                        mk_x[:, 0:128], ps_w[:, 256:384]), waits=[xt2])
                    E("vector", lambda e: e.tensor_copy(
                        mk_x[:, 128:256], ps_w[:, 384:512]), inc=True)
                    prev = tok("vector")
                E("tensor", lambda e: e.matmul(ps_w[:, 0:256], mk_x[:, 128:256],
                                               mk_r[:], start=True, stop=True),
                  waits=[prev], inc=True)
                wt_ = tok("tensor")
                E("vector", lambda e: e.tensor_copy(mk_w[:], ps_w[:, 0:256]),
                  waits=[wt_], inc=True)
                w_tok = tok("vector")

                for bi, dtl in enumerate(dts):
                    l_ap = qv(0, dtl, coff, coff + 128)
                    E("tensor", lambda e, l=l_ap, bi=bi: e.matmul(
                        ps_w[:, 256:512], l, mk_s[:, bi * 256:(bi + 1) * 256],
                        start=(bi == 0), stop=False),
                      waits=[w_tok] if bi == 0 else ())
                E("tensor", lambda e: e.matmul(
                    ps_w[:, 256:512], mk_p[:], mk_w[:], start=False, stop=True),
                  inc=True)
                ot_ = tok("tensor")
                E("vector", lambda e: e.tensor_copy(mk_tm[:, 512:768],
                                                    ps_w[:, 256:512]),
                  waits=[ot_], inc=True)
                oevt = tok("vector")
                E("scalar", lambda e: e.activation(mk_dump[:, 0:DH],
                                                   mk_tm[:, 512:768], AF.Square,
                                                   accum_out=smc("oss")),
                  waits=[oevt], inc=True)
                osq_tok = tok("scalar")
                E("vector", lambda e: e.tensor_tensor(smc("tmp"), smc("sq"),
                                                      smc("sq"), ALU.mult),
                  waits=[osq_tok])
                E("vector", lambda e: e.memset(smc("pad"), 0.0))
                E("vector", lambda e: e.memset(smc("pad"), 0.0))
                E("vector", lambda e: e.scalar_tensor_tensor(
                    smc("zz"), smc("tmp"), 1.0 / DH, smc("oss"),
                    ALU.mult, ALU.mult), inc=True)
                vt = tok("vector")
                E("scalar", lambda e: e.activation(smc("tmp2"), smc("zz"), AF.Ln,
                                                   bias=cc("eps5"), scale=cc("one")),
                  waits=[vt])
                E("scalar", lambda e: e.activation(smc("zz"), smc("tmp2"), AF.Exp,
                                                   bias=cc("zero"), scale=cc("nhalf")),
                  inc=True)
                st = tok("scalar")
                E("vector", lambda e: e.tensor_scalar(
                    mk_tm[:, 512:768], mk_tm[:, 512:768], smc("sq"), smc("zz"),
                    ALU.mult, ALU.mult), waits=[st])
                E("vector", lambda e: e.tensor_tensor(
                    mk_tm[:, 512:768], mk_tm[:, 512:768], onw_t[:], ALU.mult),
                  inc=True)
                on_tok = tok("vector")
                for bi in range(2):
                    E("tensor", lambda e, bi=bi, h=h: e.transpose(
                        ps_t[:, h * 256 + bi * 128: h * 256 + (bi + 1) * 128],
                        mk_tm[:, 512 + bi * 128: 512 + (bi + 1) * 128],
                        identb_t[:]),
                      waits=[on_tok] if bi == 0 else (), inc=(bi == 1))
                tt3 = tok("tensor")
                E("vector", lambda e, h=h, ci=ci: e.tensor_copy(
                    mk_ot[:, ci * D + h * 256: ci * D + (h + 1) * 256],
                    ps_t[:, h * 256:(h + 1) * 256]), waits=[tt3], inc=True)
                head_gate = tok("vector")
                ps_t_free = head_gate

                for bi in range(2):
                    E("tensor", lambda e, bi=bi: e.matmul(
                        ps_mm[:, bi * 256:(bi + 1) * 256],
                        mk_tm[:, bi * 128:bi * 128 + 128], mk_w[:],
                        start=True, stop=True),
                      waits=[w_tok, supd_gate[0]] if bi == 0 else (),
                      inc=(bi == 1))
                sut = tok("tensor")
                if c_ > 0:
                    E("vector", lambda e, h=h: e.tensor_tensor(
                        s_sb[h][:], s_sb[h][:], ps_mm[:, 0:512], ALU.add),
                      waits=[sut], inc=True)
                else:
                    E("vector", lambda e, h=h: e.tensor_copy(
                        s_sb[h][:], ps_mm[:, 0:512]), waits=[sut], inc=True)
                supd_tok[h] = tok("vector")
                supd_gate[0] = supd_tok[h]

        resid_gate = None
        for ci in range(CPS):
            c_ = s * CPS + ci
            xc_ap = xres[:, c_ * D:(c_ + 1) * D]
            for sl in range(2):
                for k in range(NT):
                    E("tensor", lambda e, sl=sl, k=k, ci=ci: e.matmul(
                        ps_mm[:, sl * 512:(sl + 1) * 512],
                        mk_ot[:, ci * D + k * 128: ci * D + (k + 1) * 128],
                        wo_buf[:, k * D + sl * 512: k * D + sl * 512 + 512],
                        start=(k == 0), stop=(k == NT - 1)),
                      waits=[wo_tok, head_gate, resid_gate, supd_gate[0]]
                      if (sl == 0 and k == 0) else (),
                      inc=(sl == 1 and k == NT - 1))
            mt = tok("tensor")
            E("vector", lambda e, xc_ap=xc_ap: e.tensor_tensor(
                xc_ap, xc_ap, ps_mm[:], ALU.add), waits=[mt], inc=True)
            resid2.append(tok("vector"))
            resid_gate = resid2[-1]

    ffn_phase([resid2[-1]] * NCH, store_out=True)

    final_cnt = cnt["gpsimd"]
    E("gpsimd", None, waits=[("gpsimd", final_cnt)])

    def mk(eng_name):
        def body(eng):
            for fn, waits, inc in ops[eng_name]:
                for (we, wc) in waits:
                    if we == eng_name and eng_name in COMPUTE:
                        continue
                    eng.wait_ge(sems[we], wc)
                if fn is None:
                    continue
                ins = fn(eng)
                if inc:
                    step = 16 if eng_name in ("sync", "gpsimd") else 1
                    if ins is not None:
                        ins.then_inc(sems[eng_name], step)
                    else:
                        eng.sem_inc(sems[eng_name], step)
        return body

    blk.sync(mk("sync"))
    blk.gpsimd(mk("gpsimd"))
    blk.vector(mk("vector"))
    blk.scalar(mk("scalar"))
    blk.tensor(mk("tensor"))
    stack.close()
    return nc


def kernel(**inputs):
    import ml_dtypes
    BF = ml_dtypes.bfloat16

    x = np.ascontiguousarray(np.asarray(inputs["L0_attn_hidden_states"],
                                        dtype=np.float32))
    B = x.shape[0]
    wq = np.ascontiguousarray(np.asarray(inputs["L0_attn_Wq"], np.float32))
    wk = np.ascontiguousarray(np.asarray(inputs["L0_attn_Wk"], np.float32))
    wv = np.ascontiguousarray(np.asarray(inputs["L0_attn_Wv"], np.float32))
    wo = np.ascontiguousarray(np.asarray(inputs["L0_attn_Wo"], np.float32))
    cwq = np.asarray(inputs["L0_attn_conv_wq"], np.float32)
    cwk = np.asarray(inputs["L0_attn_conv_wk"], np.float32)
    cwv = np.asarray(inputs["L0_attn_conv_wv"], np.float32)
    onw = np.asarray(inputs["L0_attn_onorm_w"], np.float32)
    fw = np.ascontiguousarray(np.asarray(inputs["L0_ffn_weight"], np.float32))
    fb = np.asarray(inputs["L0_ffn_bias"], np.float32)
    dw = np.ascontiguousarray(np.asarray(inputs["L0_ffn_delta_w"], np.float32))
    da = np.asarray(inputs["L0_ffn_delta_act"], np.float32)

    convdiag = np.zeros((3 * NT * 4, 128, 128), np.float32)
    for mi, cw in enumerate((cwq, cwk, cwv)):
        for dt in range(NT):
            for tap in range(4):
                idx = (mi * NT + dt) * 4 + tap
                np.fill_diagonal(convdiag[idx], cw[dt * 128:(dt + 1) * 128, tap])
    convdiag = convdiag.reshape(3 * NT * 4 * 128, 128)

    ib = np.eye(128, dtype=np.float32)
    pow_row = np.array([2.0 ** e for e in range(ELO + 1, ELO + 1 + NPOW)],
                       np.float32)
    powcube = np.broadcast_to(pow_row, (128, NB, NPOW)).reshape(128, NB * NPOW)

    common = {
        "wq": wq, "wk": wk, "wv": wv, "wo": wo, "fw": fw,
        "fbias": fb.reshape(1, D),
        "dw": dw,
        "da": np.broadcast_to(da.reshape(1, 1), (128, 1)),
        "onw": np.broadcast_to(onw.reshape(1, DH), (128, DH)),
        "convdiag": convdiag,
        "identb": ib.astype(BF),
        "lmask": np.tril(np.ones((128, 128), np.float32), -1).astype(BF),
        "umaski": np.triu(np.ones((128, 128), np.float32), 0).astype(BF),
        "imat": ib.astype(BF),
        "i2mat": (2.0 * ib).astype(BF),
        "powcube": np.ascontiguousarray(powcube).astype(BF),
        "onesb": np.ones((1, SLAB), np.float32).astype(BF),
    }
    common = {k: np.ascontiguousarray(v) for k, v in common.items()}

    nc = build_program()
    in_maps = []
    for b in range(B):
        m = dict(common)
        m["x"] = np.ascontiguousarray(x[b])
        in_maps.append(m)
    res = run_bass_kernel_spmd(nc, in_maps, core_ids=list(range(8)))
    out = np.stack([np.asarray(res.results[b]["out"]) for b in range(B)], axis=0)
    return out.astype(np.float32)
